# version 03
# baseline (speedup 1.0000x reference)
import math

import numpy as np
import ml_dtypes

import concourse.bass as bass
import concourse.mybir as mybir
from concourse.bass_utils import run_bass_kernel_spmd

F32 = mybir.dt.float32
BF16 = mybir.dt.bfloat16
AF = mybir.ActivationFunctionType
ALU = mybir.AluOpType
AX = mybir.AxisListType

B, S, D, WID, NH, PD = 4, 4096, 1024, 4096, 8, 128
STD = [1.0, 2.0, 4.0, 8.0, 1.0, 2.0, 4.0, 8.0]
OFF = [-1.0, -2.0, -4.0, -8.0, -1.0, -2.0, -4.0, -8.0]
EPS = 1e-5
NT = 10
NQT = 8


class Ev:

    __slots__ = ("key", "val")

    def __init__(self, key, val):
        self.key = key
        self.val = val


class Emitter:

    DMA_KEYS = ("x", "w", "o")
    QUEUE_KEYS = {"sync": ("x", "o"), "gpsimd": ("w",)}
    BATCH = {"sync": 1, "gpsimd": 4}

    def __init__(self):
        self.ops = {"sync": [], "gpsimd": [], "tensor": [], "vector": [], "scalar": []}
        self.cnt = {"x": 0, "w": 0, "t": 0, "v": 0, "s": 0, "o": 0, "g": 0}
        self.pending = {k: [] for k in self.ops}
        self.since_drain = {k: 0 for k in self.ops}

    def op(self, eng, waits, build, inc=None, drain=False):
        waits = [Ev(*w) if isinstance(w, tuple) else w for w in waits if w is not None]
        ev = None
        if inc:
            key, by = inc
            if key in self.DMA_KEYS:
                if self.since_drain[eng] >= self.BATCH[eng]:
                    for k in self.QUEUE_KEYS[eng]:
                        if self.cnt[k] > 0:
                            waits = waits + [Ev(k, self.cnt[k])]
                    for pev in self.pending[eng]:
                        pev.val = self.cnt[pev.key]
                    self.pending[eng] = []
                    self.since_drain[eng] = 0
                self.since_drain[eng] += 1
            self.cnt[key] += by
            ev = Ev(key, self.cnt[key])
            if key in self.DMA_KEYS:
                self.pending[eng].append(ev)

        def closure(eng, _w=waits, _build=build, _inc=inc, _drain=drain):
            if _drain:
                eng.drain()
            seen = {}
            for e_ in _w:
                if seen.get(e_.key, 0) < e_.val:
                    seen[e_.key] = e_.val
            for k, v in seen.items():
                eng.wait_ge(closure.sems[k], v)
            inst = _build(eng)
            if _inc is not None:
                inst.then_inc(closure.sems[_inc[0]], _inc[1])

        self.ops[eng].append(closure)
        return ev

    def finalize(self):
        for eng, evs in self.pending.items():
            for ev in evs:
                ev.val = self.cnt[ev.key]
            self.pending[eng] = []


def build_nc(dbg=False):
    nc = bass.Bass("TRN2", target_bir_lowering=False, debug=False)

    xp = nc.dram_tensor("xp", [NT * 128 + 1024, 1024], F32, kind="ExternalInput").ap()
    w1 = nc.dram_tensor("w1", [D, WID], F32, kind="ExternalInput").ap()
    w2 = nc.dram_tensor("w2", [WID, D], F32, kind="ExternalInput").ap()
    win = nc.dram_tensor("win", [D, D], F32, kind="ExternalInput").ap()
    wout = nc.dram_tensor("wout", [D, D], F32, kind="ExternalInput").ap()
    b1t = nc.dram_tensor("b1t", [128, 32], F32, kind="ExternalInput").ap()
    b2t = nc.dram_tensor("b2t", [128, 1024], F32, kind="ExternalInput").ap()
    ident = nc.dram_tensor("ident", [128, 128], BF16, kind="ExternalInput").ap()
    out = nc.dram_tensor("out", [2048, 1024], F32, kind="ExternalOutput").ap()
    if dbg:
        d_xn = nc.dram_tensor("d_xn", [128, 8 * 1280], BF16, kind="ExternalOutput").ap()
        d_v = nc.dram_tensor("d_v", [128, 10240], BF16, kind="ExternalOutput").ap()
        d_att = nc.dram_tensor("d_att", [128, 8192], BF16, kind="ExternalOutput").ap()
        d_h = nc.dram_tensor("d_h", [128, 16384], BF16, kind="ExternalOutput").ap()
        d_stat = nc.dram_tensor("d_stat", [128, 44], F32, kind="ExternalOutput").ap()
        d_probs = nc.dram_tensor("d_probs", [128, 24 * 128], BF16, kind="ExternalOutput").ap()
        d_wint = nc.dram_tensor("d_wint", [128, 8192], BF16, kind="ExternalOutput").ap()
        d_w1b = nc.dram_tensor("d_w1b", [128, 1024], BF16, kind="ExternalOutput").ap()

    xn_fm = nc.alloc_sbuf_tensor("xn_fm", [128, 8 * 1280], BF16).ap()
    arena = nc.alloc_sbuf_tensor("arena", [128, 24576], BF16).ap()
    ATT0 = 0
    V0 = 8192
    H0C = 8192
    WINTMP = 0
    WOUTTMP = 8192
    wintT = nc.alloc_sbuf_tensor("wintT", [128, 8192], BF16).ap()
    woutT = nc.alloc_sbuf_tensor("woutT", [128, 8192], BF16).ap()
    w2_sb = nc.alloc_sbuf_tensor("w2_sb", [128, 32 * 1024], BF16).ap()
    w1buf = [nc.alloc_sbuf_tensor(f"w1b{i}", [128, 1024], BF16).ap() for i in range(4)]
    probsT = nc.alloc_sbuf_tensor("probsT", [128, 24 * 128], BF16).ap()
    xsl = [nc.alloc_sbuf_tensor(f"xsl{i}", [128, 1024], F32).ap() for i in range(2)]
    xr = [nc.alloc_sbuf_tensor(f"xr{i}", [128, 1024], F32).ap() for i in range(2)]
    xn_tm = [nc.alloc_sbuf_tensor(f"xntm{i}", [128, 1024], BF16).ap() for i in range(2)]
    b2_sb = nc.alloc_sbuf_tensor("b2_sb", [128, 1024], F32).ap()
    b1_sb = nc.alloc_sbuf_tensor("b1_sb", [128, 32], F32).ap()
    id_sb = nc.alloc_sbuf_tensor("id_sb", [128, 128], BF16).ap()
    g_sb = nc.alloc_sbuf_tensor("g_sb", [128, 128], F32).ap()
    t2_sb = nc.alloc_sbuf_tensor("t2_sb", [128, 128], F32).ap()
    tscr = [nc.alloc_sbuf_tensor(f"tscr{i}", [128, 128], F32).ap() for i in range(2)]
    e_sb = [nc.alloc_sbuf_tensor(f"e_sb{i}", [128, 128], F32).ap() for i in range(2)]
    pm_sb = nc.alloc_sbuf_tensor("pm_sb", [128, 128], F32).ap()
    stat = nc.alloc_sbuf_tensor("stat", [128, 44], F32).ap()

    pd = [nc.alloc_psum_tensor(f"pd{i}", [128, 1024], F32).ap() for i in range(4)]
    pd_bf = [p.bitcast(BF16) for p in pd]

    E = Emitter()
    T, V, SC, SY, G = "tensor", "vector", "scalar", "sync", "gpsimd"

    slot = {}

    def use(name, ev):
        slot.setdefault(name, []).append(ev)

    def take(name):
        evs = slot.get(name, [])
        slot[name] = []
        return evs

    ev_b1 = E.op(SY, [], lambda e: e.dma_start(b1_sb[:, :], b1t[:, :]), inc=("x", 16))
    ev_b2 = E.op(SY, [], lambda e: e.dma_start(b2_sb[:, :], b2t[:, :]), inc=("x", 16))
    ev_id = E.op(SY, [], lambda e: e.dma_start(id_sb[:, :], ident[:, :]), inc=("x", 16))

    ev_g = E.op(G, [], lambda e: e.iota(g_sb[:, :], pattern=[[-1, 128]], base=0,
                                        channel_multiplier=1,
                                        allow_small_or_imprecise_dtypes=True), inc=("g", 1))
    ev_win = E.op(G, [], lambda e: e.dma_start(
        arena[:, WINTMP:WINTMP + 8192].rearrange("p (c n) -> p c n", c=8),
        win.rearrange("(c p) n -> p c n", p=128)), inc=("w", 16))
    ev_wout = E.op(G, [], lambda e: e.dma_start(
        arena[:, WOUTTMP:WOUTTMP + 8192].rearrange("p (c n) -> p c n", c=8),
        wout.rearrange("(c p) n -> p c n", p=128)), inc=("w", 16))
    ev_w2 = None
    for i in range(4):
        ev_w2 = E.op(G, [], lambda e, i=i: e.dma_start(
            w2_sb[:, i * 8192:(i + 1) * 8192].rearrange("p (c n) -> p c n", c=8),
            w2.rearrange("(c p) n -> p c n", p=128)[:, i * 8:(i + 1) * 8, :]), inc=("w", 16))

    ev_probs = None
    for h in range(NH):
        inv64 = 1.0 / (STD[h] * math.sqrt(2.0 * math.pi))
        hi = np.float32(inv64)
        lo = np.float32(inv64 - np.float64(hi))
        for di, delta in enumerate((-1, 0, 1)):
            idx = h * 3 + di
            sc = np.float32(1.0 / STD[h])
            bi = np.float32((128.0 * delta - OFF[h]) / STD[h])
            eb = e_sb[idx % 2]
            tb = tscr[idx % 2]
            w_e = take(f"e{idx % 2}")
            ev_t = E.op(V, [ev_g] + take(f"tscr{idx % 2}"),
                        lambda e, sc=sc, bi=bi, tb=tb: e.tensor_scalar(
                            tb[:, :], g_sb[:, :], float(sc), float(bi),
                            op0=ALU.mult, op1=ALU.add), inc=("v", 1))
            E.op(SC, [ev_t] + w_e, lambda e, tb=tb: e.activation(
                t2_sb[:, :], tb[:, :], AF.Square), drain=True)
            ev_e = E.op(SC, [], lambda e, eb=eb: e.activation(
                eb[:, :], t2_sb[:, :], AF.Exp, scale=-0.5), inc=("s", 1), drain=True)
            use(f"tscr{idx % 2}", ev_e)
            E.op(V, [ev_e], lambda e, eb=eb, hi=hi: e.tensor_scalar(
                pm_sb[:, :], eb[:, :], float(hi), None, op0=ALU.mult))
            ev_probs = E.op(V, [], lambda e, eb=eb, lo=lo, idx=idx: e.scalar_tensor_tensor(
                probsT[:, idx * 128:(idx + 1) * 128], eb[:, :], float(lo), pm_sb[:, :],
                op0=ALU.mult, op1=ALU.add), inc=("v", 1), drain=True)
            use(f"e{idx % 2}", ev_probs)

    ev_wTe = {}
    for wi, (base, dst, ev_ld) in enumerate(((WINTMP, wintT, ev_win), (WOUTTMP, woutT, ev_wout))):
        for ic in range(8):
            pslot = (wi * 8 + ic) % 2
            waits = [ev_ld, ev_id] + take(f"pd{pslot}")
            ev_T = None
            for jc in range(8):
                ev_T = E.op(T, waits if jc == 0 else [],
                            lambda e, base=base, ic=ic, jc=jc, pslot=pslot: e.transpose(
                                pd_bf[pslot][:, jc * 128:(jc + 1) * 128],
                                arena[:, base + ic * 1024 + jc * 128: base + ic * 1024 + (jc + 1) * 128],
                                id_sb[:, :]),
                            inc=("t", 1) if jc == 7 else None)
            ev_ev = E.op(V, [ev_T], lambda e, dst=dst, ic=ic, pslot=pslot: e.tensor_copy(
                dst.rearrange("p (c n) -> p c n", c=8)[:, :, ic * 128:(ic + 1) * 128],
                pd_bf[pslot][:, 0:1024].rearrange("p (c n) -> p c n", c=8)), inc=("v", 1))
            ev_wTe[(wi, ic)] = ev_ev
            use(f"pd{pslot}", ev_ev)
    ev_winT_done = ev_wTe[(0, 7)]
    ev_woutT_done = ev_wTe[(1, 7)]
    if dbg:
        E.op(SY, [ev_probs], lambda e: e.dma_start(d_probs[:, :], probsT[:, :]), inc=("o", 16))
        E.op(SY, [ev_winT_done], lambda e: e.dma_start(d_wint[:, :], wintT[:, :]), inc=("o", 16))

    ev_apply = {}
    ev_store = {}
    ev_stt = {}
    ev_sq = {}

    for H in range(2):
        ev_x = {}
        ev_pdevac = {}
        ev_xnT = {}
        for t in range(NT):
            tile = 8 * H + t
            waits = take(f"xsl{t % 2}")
            ev_x[t] = E.op(SY, waits, lambda e, tile=tile, t=t: e.dma_start(
                xsl[t % 2][:, :], xp[tile * 128:(tile + 1) * 128, :]), inc=("x", 16))

            E.op(V, [ev_x[t]], lambda e, t=t: e.reduce_sum(
                stat[:, 4 * t:4 * t + 1], xsl[t % 2][:, :], axis=AX.X))
            sq_waits = [ev_x[t]] + (take("xr0_sq") if t == 0 else [])
            ev_sq[(H, t)] = E.op(SC, sq_waits, lambda e, t=t: e.activation(
                xr[0][:, :], xsl[t % 2][:, :], AF.Square, accum_out=stat[:, 4 * t + 1:4 * t + 2]),
                inc=("s", 1), drain=True)
            E.op(V, [ev_sq[(H, t)]], lambda e, t=t: e.tensor_scalar(
                stat[:, 4 * t + 2:4 * t + 3], stat[:, 4 * t:4 * t + 1], -1.0 / 1024.0, None, op0=ALU.mult),
                drain=True)
            E.op(V, [], lambda e, t=t: e.tensor_mul(
                stat[:, 40:41], stat[:, 4 * t + 2:4 * t + 3], stat[:, 4 * t + 2:4 * t + 3]), drain=True)
            E.op(V, [], lambda e, t=t: e.tensor_scalar(
                stat[:, 4 * t + 1:4 * t + 2], stat[:, 4 * t + 1:4 * t + 2], 1.0 / 1024.0, None, op0=ALU.mult))
            E.op(V, [], lambda e, t=t: e.tensor_sub(
                stat[:, 4 * t + 1:4 * t + 2], stat[:, 4 * t + 1:4 * t + 2], stat[:, 40:41]), drain=True)
            ev_ve = E.op(V, [], lambda e, t=t: e.tensor_scalar(
                stat[:, 4 * t + 1:4 * t + 2], stat[:, 4 * t + 1:4 * t + 2], float(np.float32(EPS)), None,
                op0=ALU.add), inc=("v", 1), drain=True)
            E.op(SC, [ev_ve], lambda e, t=t: e.activation(
                stat[:, 4 * t + 3:4 * t + 4], stat[:, 4 * t + 1:4 * t + 2], AF.Ln), drain=True)
            ev_r0 = E.op(SC, [], lambda e, t=t: e.activation(
                stat[:, 4 * t + 3:4 * t + 4], stat[:, 4 * t + 3:4 * t + 4], AF.Exp, scale=-0.5),
                inc=("s", 1), drain=True)
            nw = [ev_r0]
            for _ in range(2):
                E.op(V, nw, lambda e, t=t: e.tensor_mul(
                    stat[:, 40:41], stat[:, 4 * t + 3:4 * t + 4], stat[:, 4 * t + 3:4 * t + 4]),
                    drain=True)
                nw = []
                E.op(V, [], lambda e, t=t: e.tensor_mul(
                    stat[:, 40:41], stat[:, 40:41], stat[:, 4 * t + 1:4 * t + 2]), drain=True)
                E.op(V, [], lambda e, t=t: e.tensor_scalar(
                    stat[:, 40:41], stat[:, 40:41], -0.5, 1.5, op0=ALU.mult, op1=ALU.add), drain=True)
                E.op(V, [], lambda e, t=t: e.tensor_mul(
                    stat[:, 4 * t + 3:4 * t + 4], stat[:, 4 * t + 3:4 * t + 4], stat[:, 40:41]),
                    drain=True)
            aw = take(f"xntm{t % 2}")
            ev_apply[(H, t)] = E.op(V, aw, lambda e, t=t: e.tensor_scalar(
                xn_tm[t % 2][:, :], xsl[t % 2][:, :], stat[:, 4 * t + 2:4 * t + 3],
                stat[:, 4 * t + 3:4 * t + 4], op0=ALU.add, op1=ALU.mult), inc=("v", 1), drain=True)
            use(f"xsl{t % 2}", ev_apply[(H, t)])
            use(f"xsl{t % 2}", ev_sq[(H, t)])

            tw = [ev_apply[(H, t)]] + take(f"pd{t % 2}")
            for dc in range(8):
                ev_xnT[t] = E.op(T, tw if dc == 0 else [],
                                 lambda e, t=t, dc=dc: e.transpose(
                                     pd_bf[t % 2][:, dc * 128:(dc + 1) * 128],
                                     xn_tm[t % 2][:, dc * 128:(dc + 1) * 128], id_sb[:, :]),
                                 inc=("t", 1) if dc == 7 else None)
            use(f"xntm{t % 2}", ev_xnT[t])
            ev_pdevac[t] = E.op(V, [ev_xnT[t]], lambda e, t=t: e.tensor_copy(
                xn_fm.rearrange("p (c n) -> p c n", c=8)[:, :, t * 128:(t + 1) * 128],
                pd_bf[t % 2][:, 0:1024].rearrange("p (c n) -> p c n", c=8)), inc=("v", 1))
            use(f"pd{t % 2}", ev_pdevac[t])

        ev_dxn = None
        if dbg and H == 0:
            ev_dxn = E.op(SY, [ev_pdevac[9]], lambda e: e.dma_start(d_xn[:, :], xn_fm[:, :]), inc=("o", 16))
            E.op(SY, [ev_pdevac[9]], lambda e: e.dma_start(d_stat[:, :], stat[:, :]), inc=("o", 16))

        ev_vevac = {}
        for t in range(NT):
            pslot = 2 + t % 2
            waits = [ev_pdevac[t], ev_winT_done, ev_dxn] + take(f"pd{pslot}")
            ev_mm = None
            for jc in range(8):
                for gq in range(2):
                    first = (jc == 0)
                    last = (jc == 7)
                    ev_mm = E.op(T, waits if (jc == 0 and gq == 0) else [],
                                 lambda e, t=t, jc=jc, gq=gq, pslot=pslot, first=first, last=last: e.matmul(
                                     pd[pslot][:, gq * 512:(gq + 1) * 512],
                                     xn_fm[:, jc * 1280 + t * 128: jc * 1280 + (t + 1) * 128],
                                     wintT[:, jc * 1024 + gq * 512: jc * 1024 + (gq + 1) * 512],
                                     start=first, stop=last),
                                 inc=("t", 1) if (jc == 7 and gq == 1) else None)
            ev_vevac[t] = E.op(V, [ev_mm], lambda e, t=t, pslot=pslot: e.tensor_copy(
                arena[:, V0 + t * 1024: V0 + (t + 1) * 1024], pd[pslot][:, :]), inc=("v", 1))
            use(f"pd{pslot}", ev_vevac[t])

        ev_dv = None
        if dbg and H == 0:
            ev_dv = E.op(SY, [ev_vevac[9]], lambda e: e.dma_start(
                d_v[:, :], arena[:, V0:V0 + 10240]), inc=("o", 16))

        ev_att = {}
        for h in range(NH):
            pslot = h % 2
            waits = [ev_vevac[9], ev_probs, ev_dv] + take(f"pd{pslot}")
            evA = evB = None
            for qt in range(NQT):
                for di in range(3):
                    kv = qt + di
                    last = (di == 2)
                    ev = E.op(T, waits if (qt == 0 and di == 0) else [],
                              lambda e, h=h, qt=qt, di=di, kv=kv, pslot=pslot, last=last: e.matmul(
                                  pd[pslot][:, qt * 128:(qt + 1) * 128],
                                  arena[:, V0 + kv * 1024 + h * 128: V0 + kv * 1024 + (h + 1) * 128],
                                  probsT[:, (h * 3 + di) * 128:(h * 3 + di + 1) * 128],
                                  start=(di == 0), stop=last),
                              inc=("t", 1) if (last and qt in (3, 7)) else None)
                    if last and qt == 3:
                        evA = ev
                    if last and qt == 7:
                        evB = ev
            ev_attA = E.op(SC, [evA], lambda e, h=h, pslot=pslot: e.copy(
                arena[:, ATT0 + h * 1024: ATT0 + h * 1024 + 512], pd[pslot][:, 0:512]), inc=("s", 1))
            ev_attB = E.op(SC, [evB], lambda e, h=h, pslot=pslot: e.copy(
                arena[:, ATT0 + h * 1024 + 512: ATT0 + h * 1024 + 1024], pd[pslot][:, 512:1024]),
                inc=("s", 1))
            ev_att[h] = ev_attB
            use(f"pd{pslot}", ev_attA)
            use(f"pd{pslot}", ev_attB)

        for Q in range(2):
            ev_w1 = {}
            ev_f1 = {}
            ev_relu = {}
            for wc in range(32):
                bslot = wc % 4
                pslot = 2 + bslot // 2
                sub = bslot % 2
                ww = take(f"w1b{bslot}")
                ev_w1[wc] = E.op(G, ww, lambda e, wc=wc: e.dma_start(
                    w1buf[wc % 4][:, :].rearrange("p (c n) -> p c n", c=8),
                    w1.rearrange("(c p) n -> p c n", p=128)[:, :, wc * 128:(wc + 1) * 128]),
                    inc=("w", 16))
                if dbg and H == 0 and Q == 0 and wc == 0:
                    evd = E.op(SY, [ev_w1[0]], lambda e: e.dma_start(
                        d_w1b[:, :], w1buf[0][:, :]), inc=("o", 16))
                    use("w1b0", evd)
                waits = [ev_w1[wc], ev_b1] + take(f"ph{bslot}") + take(f"pd{pslot}")
                ev = None
                for dc in range(8):
                    ev = E.op(T, waits if dc == 0 else [],
                              lambda e, wc=wc, dc=dc, Q=Q, pslot=pslot, sub=sub: e.matmul(
                                  pd[pslot][:, sub * 512:(sub + 1) * 512],
                                  w1buf[wc % 4][:, dc * 128:(dc + 1) * 128],
                                  xn_fm[:, dc * 1280 + 128 + Q * 512: dc * 1280 + 128 + (Q + 1) * 512],
                                  start=(dc == 0), stop=(dc == 7)),
                              inc=("t", 1) if dc == 7 else None)
                ev_f1[wc] = ev
                use(f"w1b{bslot}", ev)
                ev_relu[wc] = E.op(SC, [ev], lambda e, wc=wc, pslot=pslot, sub=sub: e.activation(
                    arena[:, H0C + wc * 512: H0C + (wc + 1) * 512],
                    pd[pslot][:, sub * 512:(sub + 1) * 512], AF.Relu,
                    bias=b1_sb[:, wc:wc + 1]), inc=("s", 1))
                use(f"ph{bslot}", ev_relu[wc])
                use(f"pd{pslot}", ev_relu[wc])

            if dbg and H == 0 and Q == 0:
                E.op(SY, [ev_att[7]], lambda e: e.dma_start(
                    d_att[:, :], arena[:, ATT0:ATT0 + 8192]), inc=("o", 16))
                E.op(SY, [ev_relu[31]], lambda e: e.dma_start(
                    d_h[:, :], arena[:, H0C:H0C + 16384]), inc=("o", 16))

            ev_xr = {}
            for tt in range(4):
                T_half = Q * 4 + tt
                tile = 1 + 8 * H + T_half
                xw = take(f"xr{tt % 2}")
                if Q == 0 and tt == 0:
                    xw = xw + [ev_sq[(H, 9)]]
                ev_xr[tt] = E.op(SY, xw, lambda e, tile=tile, tt=tt: e.dma_start(
                    xr[tt % 2][:, :], xp[tile * 128:(tile + 1) * 128, :]), inc=("x", 16))
                pslot = tt % 2
                waits = [ev_relu[31], ev_att[7], ev_woutT_done, ev_w2] + take(f"pd{pslot}")
                ev = None
                n_src = [("att", dc) for dc in range(8)] + [("h", wc) for wc in range(32)]
                for si, (kind, ci) in enumerate(n_src):
                    for gq in range(2):
                        first = (si == 0)
                        last = (si == len(n_src) - 1)
                        if kind == "att":
                            statn = lambda e, ci=ci, T_half=T_half: arena[:, ci * 1024 + T_half * 128: ci * 1024 + (T_half + 1) * 128]
                            movn = lambda e, ci=ci, gq=gq: woutT[:, ci * 1024 + gq * 512: ci * 1024 + (gq + 1) * 512]
                        else:
                            statn = lambda e, ci=ci, tt=tt: arena[:, H0C + ci * 512 + tt * 128: H0C + ci * 512 + (tt + 1) * 128]
                            movn = lambda e, ci=ci, gq=gq: w2_sb[:, ci * 1024 + gq * 512: ci * 1024 + (gq + 1) * 512]
                        ev = E.op(T, waits if (si == 0 and gq == 0) else [],
                                  lambda e, statn=statn, movn=movn, gq=gq, pslot=pslot, first=first, last=last: e.matmul(
                                      pd[pslot][:, gq * 512:(gq + 1) * 512],
                                      statn(e), movn(e), start=first, stop=last),
                                  inc=("t", 1) if (last and gq == 1) else None)
                ow = take(f"xsl{tt % 2}")
                E.op(V, [ev_xr[tt], ev_b2] + ow, lambda e, tt=tt: e.tensor_add(
                    xsl[tt % 2][:, :], xr[tt % 2][:, :], b2_sb[:, :]))
                ev_stt[(H, Q, tt)] = E.op(V, [ev], lambda e, tt=tt, pslot=pslot: e.scalar_tensor_tensor(
                    xsl[tt % 2][:, :], pd[pslot][:, :], 1.0, xsl[tt % 2][:, :],
                    op0=ALU.mult, op1=ALU.add), inc=("v", 1), drain=True)
                use(f"xr{tt % 2}", ev_stt[(H, Q, tt)])
                use(f"pd{pslot}", ev_stt[(H, Q, tt)])
                ev_store[(H, Q, tt)] = E.op(SY, [ev_stt[(H, Q, tt)]], lambda e, H=H, T_half=T_half, tt=tt: e.dma_start(
                    out[(8 * H + T_half) * 128:(8 * H + T_half + 1) * 128, :], xsl[tt % 2][:, :]),
                    inc=("o", 16))
                use(f"xsl{tt % 2}", ev_store[(H, Q, tt)])

        slot["xr0_sq"] = [ev_stt[(H, 1, 2)]]

    total_o = E.cnt["o"]
    E.op(SY, [("o", total_o)], lambda e: e.nop())
    E.finalize()

    with (
        nc.semaphore("sx") as sx,
        nc.semaphore("sw") as sw,
        nc.semaphore("st_") as st_,
        nc.semaphore("sv") as sv,
        nc.semaphore("ss") as ss,
        nc.semaphore("so") as so,
        nc.semaphore("sg") as sg,
        nc.Block() as block,
    ):
        sems = {"x": sx, "w": sw, "t": st_, "v": sv, "s": ss, "o": so, "g": sg}
        for eng_name, attr in (("sync", block.sync), ("gpsimd", block.gpsimd),
                               ("tensor", block.tensor), ("vector", block.vector),
                               ("scalar", block.scalar)):
            ops = E.ops[eng_name]
            for c in ops:
                c.sems = sems

            def run_all(eng, ops=ops):
                for c in ops:
                    c(eng)
            attr(run_all)
    return nc


_NC_CACHE = None


def kernel(L0_attn_values, L0_attn_input_weights, L0_attn_output_weight,
           L0_ffn_w1, L0_ffn_b1, L0_ffn_w2, L0_ffn_b2):
    global _NC_CACHE
    x = np.asarray(L0_attn_values, dtype=np.float32)
    win = np.ascontiguousarray(np.asarray(L0_attn_input_weights, dtype=np.float32))
    wout = np.ascontiguousarray(np.asarray(L0_attn_output_weight, dtype=np.float32))
    w1 = np.ascontiguousarray(np.asarray(L0_ffn_w1, dtype=np.float32))
    w2 = np.ascontiguousarray(np.asarray(L0_ffn_w2, dtype=np.float32))
    b1 = np.asarray(L0_ffn_b1, dtype=np.float32)
    b2 = np.asarray(L0_ffn_b2, dtype=np.float32)

    b1t = np.ascontiguousarray(b1.reshape(32, 128).T)
    b2t = np.ascontiguousarray(np.broadcast_to(b2[None, :], (128, 1024)))
    ident = np.eye(128, dtype=ml_dtypes.bfloat16)

    if _NC_CACHE is None:
        _NC_CACHE = build_nc()
    nc = _NC_CACHE

    in_maps = []
    for c in range(8):
        b = c // 2
        s0 = 2048 * (c % 2)
        lo, hi = s0 - 128, s0 + 2048 + 128
        xp = np.zeros((2304, 1024), dtype=np.float32)
        a, bnd = max(lo, 0), min(hi, S)
        xp[a - lo: a - lo + (bnd - a)] = x[b, a:bnd]
        in_maps.append(dict(xp=xp, w1=w1, w2=w2, win=win, wout=wout,
                            b1t=b1t, b2t=b2t, ident=ident))

    res = run_bass_kernel_spmd(nc, in_maps, core_ids=list(range(8)))
    outp = np.empty((B, S, D), dtype=np.float32)
    for c in range(8):
        b = c // 2
        s0 = 2048 * (c % 2)
        outp[b, s0:s0 + 2048] = res.results[c]["out"]
    return outp
